# version 36
# baseline (speedup 1.0000x reference)
import math
import contextlib
import numpy as np
import concourse.bass as bass
import concourse.mybir as mybir
from concourse.bass_utils import run_bass_kernel_spmd

F32 = mybir.dt.float32
F32R = mybir.dt.float32r
AF = mybir.ActivationFunctionType
ALU = mybir.AluOpType
AX = mybir.AxisListType

D = 2048
KC = D // 128
NH = 16
SCALE = 1.0 / math.sqrt(128.0)
NEG_BIG = -1e30
EPS = 1e-5

ENGINES = ("sync", "tensor", "vector", "scalar", "gpsimd")


def build_nc(shard_t, tile_t):
    n_tiles = shard_t // tile_t
    T = tile_t
    NCH = T // 128

    nc = bass.Bass("TRN2", target_bir_lowering=False)

    d_xT = nc.dram_tensor("xT", [D, shard_t], F32, kind="ExternalInput").ap()
    d_wq = nc.dram_tensor("wqT", [D, D], F32, kind="ExternalInput").ap()
    d_wk = nc.dram_tensor("wkT", [D, D], F32, kind="ExternalInput").ap()
    d_wv = nc.dram_tensor("wvT", [D, D], F32, kind="ExternalInput").ap()
    d_wo = nc.dram_tensor("woT", [D, D], F32, kind="ExternalInput").ap()
    d_bw = nc.dram_tensor("bwT", [D, D], F32, kind="ExternalInput").ap()
    d_la = nc.dram_tensor("laT", [D, 128], F32, kind="ExternalInput").ap()
    d_lb = nc.dram_tensor("lbF", [128, D], F32, kind="ExternalInput").ap()
    d_rw = nc.dram_tensor("rwT", [D, 8], F32, kind="ExternalInput").ap()
    d_bb = nc.dram_tensor("bb2", [128, KC], F32, kind="ExternalInput").ap()
    d_rb = nc.dram_tensor("rb2", [8, 1], F32, kind="ExternalInput").ap()
    d_i128 = nc.dram_tensor("id128", [128, 128], F32, kind="ExternalInput").ap()
    d_i16 = nc.dram_tensor("id16", [16, 16], F32, kind="ExternalInput").ap()
    d_i8 = nc.dram_tensor("id8", [8, 8], F32, kind="ExternalInput").ap()
    d_msk = nc.dram_tensor("maskbd", [128, 128], F32, kind="ExternalInput").ap()
    d_out = nc.dram_tensor("out", [shard_t, D], F32, kind="ExternalOutput").ap()

    with contextlib.ExitStack() as ctx:
        def sbuf(name, shape, dt=F32):
            return ctx.enter_context(nc.sbuf_tensor(name, shape, dt))

        xT = sbuf("xTs", [128, KC * T])
        bufA = sbuf("bufA", [128, NH * T])
        bufB = sbuf("bufB", [128, KC * T])
        bufC = sbuf("bufC", [128, KC * T])
        avT = sbuf("avT", [128, NH * T])
        SH = sbuf("SH", [128, 4096])
        ARENA = sbuf("ARENA", [128, 3072])
        xsp = sbuf("xsp", [128, 3 * T])
        wrawD = sbuf("wrawD", [128, 1024])
        whiB = [sbuf("whi0", [128, 512]), sbuf("whi1", [128, 512])]
        wloB = [sbuf("wlo0", [128, 512]), sbuf("wlo1", [128, 512])]
        i128 = sbuf("i128s", [128, 128])
        i8 = sbuf("i8s", [8, 8])
        AN8 = [ARENA[:, 0:512], ARENA[:, 512:1024]]
        BDs = [ARENA[:, 1024:1536], ARENA[:, 1536:2048]]
        VGs = [ARENA[:, 2048:2560], ARENA[:, 2560:3072]]
        logT2 = ARENA[0:8, 0:512]
        gexp2b = [ARENA[:, 2048 + 128 * _c:2048 + 128 * (_c + 1)]
                  for _c in range(4)]
        rb2s = ARENA[0:8, 896:897]
        gateT2 = ARENA[:, 1024:1536]
        gmidT2 = ARENA[:, 1536:2048]
        GT2 = []
        for _par in range(4):
            _p = 640 + 64 * _par
            GT2.append(dict(
                lgtok=ARENA[:, _p:_p + 8], m1=ARENA[:, _p + 8:_p + 9],
                eq1=ARENA[:, _p + 16:_p + 24], l2t=ARENA[:, _p + 24:_p + 32],
                m2=ARENA[:, _p + 32:_p + 33], ddt=ARENA[:, _p + 33:_p + 34],
                e2t=ARENA[:, _p + 34:_p + 35], dent=ARENA[:, _p + 35:_p + 36],
                rden=ARENA[:, _p + 36:_p + 37], w2s=ARENA[:, _p + 37:_p + 38],
                eq2=ARENA[:, _p + 40:_p + 48], gatev=ARENA[:, _p + 48:_p + 56]))
        xhiB = [xsp[:, 0:T], xsp[:, T:2 * T]]
        xloB = [xsp[:, 2 * T:3 * T], xsp[:, 2 * T:3 * T]]
        cb128 = sbuf("cb128", [128, 128])
        mskBD = cb128
        rsum8 = [sbuf("rsum8a", [128, 4]), sbuf("rsum8b", [128, 4])]
        rrec8 = [sbuf("rrec8a", [128, 4]), sbuf("rrec8b", [128, 4])]
        s1 = [sbuf("s1a", [128, 1]), sbuf("s1b", [128, 1])]
        s2 = [sbuf("s2a", [128, 1]), sbuf("s2b", [128, 1])]
        muv = [sbuf("muva", [128, 1]), sbuf("muvb", [128, 1])]
        msq = [sbuf("msqa", [128, 1]), sbuf("msqb", [128, 1])]
        varv = [sbuf("varva", [128, 1]), sbuf("varvb", [128, 1])]
        sdv = [sbuf("sdva", [128, 1]), sbuf("sdvb", [128, 1])]
        rstd = [sbuf("rstda", [128, 1]), sbuf("rstdb", [128, 1])]

        x1tok2 = [SH[:, 0:2048], SH[:, 2048:4096]]
        gateT = SH[:, 0:T]
        gmidT = SH[:, T:2 * T]
        logT = SH[0:8, 2 * T:3 * T]
        gexp = SH[:, 3 * T:3 * T + 128]
        rwT = cb128
        GT = []
        for _par in range(2):
            _p = 3 * T + 128 + 64 * _par
            GT.append(dict(
                lgtok=SH[:, _p:_p + 8], m1=SH[:, _p + 8:_p + 9],
                eq1=SH[:, _p + 16:_p + 24], l2t=SH[:, _p + 24:_p + 32],
                m2=SH[:, _p + 32:_p + 33], ddt=SH[:, _p + 33:_p + 34],
                e2t=SH[:, _p + 34:_p + 35], dent=SH[:, _p + 35:_p + 36],
                rden=SH[:, _p + 36:_p + 37], w2s=SH[:, _p + 37:_p + 38],
                eq2=SH[:, _p + 40:_p + 48], gatev=SH[:, _p + 48:_p + 56]))
        bb = SH[:, 3 * T + 256:3 * T + 272]
        rb = SH[0:8, 3 * T + 272:3 * T + 273]
        laT = SH[:, 2048:4096]
        lbF = SH[:, 2048:4096]
        s2q = [sbuf("s2qa", [128, 4]), sbuf("s2qb", [128, 4])]
        s1q = [sbuf("s1qa", [128, 4]), sbuf("s1qb", [128, 4])]
        epsv = sbuf("epsv", [128, 1])
        halfv = sbuf("halfv", [128, 1])

        pb = [ctx.enter_context(nc.psum_tensor(f"pb{i}", [128, 512], F32))
              for i in range(8)]

        sem = {n: ctx.enter_context(nc.semaphore(n))
               for n in ("ld", "lx", "lm", "lr", "lg", "lb", "la", "wa", "wb",
                         "wc", "pe", "dv", "se", "st", "su", "sv")}
        block = ctx.enter_context(nc.Block())

        def program(eng_name, E):
            cnt = {n: 0 for n in sem}
            bank_free = [("dv", 0)] * 8
            wbuf_sem = {0: "wa", 1: "wb"}
            wbuf_free = {0: ("pe", 0), 1: ("pe", 0), 2: ("pe", 0), 3: ("pe", 0)}
            wcount = [0]

            def emit(engine, fn, waits=(), inc=None, by=1):
                if inc is not None:
                    cnt[inc] += by
                if engine == eng_name and E is not None:
                    for s, v in waits:
                        if v > 0:
                            E.wait_ge(sem[s], v)
                    r = fn(E)
                    if inc is not None:
                        r.then_inc(sem[inc], by)
                return cnt[inc] if inc is not None else None

            def barrier_waits():
                return [(s, cnt[s]) for s in ("pe", "dv", "se", "ld", "st")]

            def barrier():
                return barrier_waits()

            for dst, src in [(i128, d_i128), (i8, d_i8)]:
                emit("sync", lambda e, dst=dst, src=src:
                     e.dma_start(out=dst[:], in_=src[:]), inc="ld", by=16)
            consts_ld = cnt["ld"]
            emit("vector", lambda e: e.memset(epsv[:], EPS), inc="dv")
            emit("vector", lambda e: e.memset(halfv[:], 0.5), inc="dv")
            eps_dv = cnt["dv"]


            whi_free = {0: ("pe", 0), 1: ("pe", 0)}
            xsp_free = {0: ("pe", 0), 1: ("pe", 0)}
            xlo_gate = [("pe", 0)]
            x1tok_free_g = [("dv", 0), ("dv", 0), ("dv", 0), ("dv", 0)]
            x_free = [("dv", 0)]

            def stream_gemm(wsrc, rhs_of_kc, evac, rhs_wait=(), bias_first=None,
                            bias_last=None, bias_wait=(), mid_cb=None,
                            w2ap=None, w2wait=()):
                pending = []
                s2_used = [False]
                for q in range(4):
                    oc0 = q * 4
                    banks = [(q % 2) * 4 + j for j in range(4)]
                    last_bank = {}
                    for ki in range(KC):
                        kc = ki
                        wbi = (q * KC + ki) % 2
                        wri = wcount[0] % 4
                        wcount[0] += 1
                        pair = wrawD if wri < 2 else w2ap
                        wsem = wbuf_sem[wri // 2]
                        if wri % 2 == 0:
                            dw = [wbuf_free[wri], wbuf_free[wri + 1]]
                            if wri == 2 and not s2_used[0]:
                                dw += list(w2wait)
                                s2_used[0] = True
                            emit("sync", lambda e, kc=kc, oc0=oc0, pair=pair:
                                 e.dma_start(
                                     out=pair.rearrange("p (c e) -> p c e", c=2),
                                     in_=wsrc(kc, oc0, 4, nkc=2)
                                     .rearrange("(c p) e -> p c e", p=128)),
                                 waits=dw, inc=wsem, by=16)
                        wready = cnt[wsem]
                        wr_ap = pair[:, (wri % 2) * 512:(wri % 2) * 512 + 512]
                        hi_m = emit("scalar", lambda e, wbi=wbi, wr_ap=wr_ap:
                            e.copy(out=whiB[wbi][:].bitcast(F32R), in_=wr_ap),
                            waits=[(wsem, wready), whi_free[wbi]], inc="se")
                        lo_m = emit("vector", lambda e, wbi=wbi, wr_ap=wr_ap:
                            e.tensor_sub(wloB[wbi][:].bitcast(F32R), wr_ap,
                                         whiB[wbi][:].bitcast(F32R)),
                            waits=[(wsem, wready), ("se", hi_m), whi_free[wbi]],
                            inc="dv")
                        xw = list(rhs_wait) if (q == 0 and ki == 0) else []
                        xhi_m = emit("scalar", lambda e, kc=kc, wbi=wbi:
                            e.copy(out=xhiB[wbi].bitcast(F32R), in_=rhs_of_kc(kc)),
                            waits=xw + [xsp_free[wbi]], inc="se")
                        xlo_m = emit("vector", lambda e, kc=kc, wbi=wbi:
                            e.tensor_sub(xloB[wbi].bitcast(F32R), rhs_of_kc(kc),
                                         xhiB[wbi].bitcast(F32R)),
                            waits=[("se", xhi_m), xsp_free[wbi], xlo_gate[0]],
                            inc="dv")
                        if q == 0 and ki == 2 and mid_cb is not None:
                            mid_cb()
                        if ki == 2 and pending:
                            for fnp in pending:
                                fnp()
                            pending = []
                        w0 = [("se", hi_m), ("se", xhi_m),
                              ("dv", lo_m), ("dv", xlo_m)]
                        if ki == 0:
                            for j in range(4):
                                w0.append(bank_free[banks[j]])
                            if bias_first is not None:
                                for j in range(4):
                                    emit("tensor", lambda e, j=j, oc0=oc0, b=banks[j]:
                                         bias_first(e, oc0 + j, pb[b][0:128, 0:T]),
                                         waits=(list(w0) if j == 0 else []), inc="pe")
                        st = (ki == 0 and bias_first is None)
                        for j in range(4):
                            emit("tensor", lambda e, j=j, wbi=wbi, st=st, b=banks[j]:
                                e.matmul(pb[b][0:128, 0:T],
                                         whiB[wbi][:].bitcast(F32R)[:, j * 128:(j + 1) * 128],
                                         xhiB[wbi].bitcast(F32R),
                                         start=st, stop=False),
                                waits=(w0 if j == 0 else []), inc="pe")
                        for j in range(4):
                            emit("tensor", lambda e, j=j, wbi=wbi, b=banks[j]:
                                e.matmul(pb[b][0:128, 0:T],
                                         whiB[wbi][:].bitcast(F32R)[:, j * 128:(j + 1) * 128],
                                         xloB[wbi].bitcast(F32R),
                                         start=False, stop=False),
                                inc="pe")
                        for j in range(4):
                            sp = (ki == KC - 1) and (bias_last is None)
                            last_bank[banks[j]] = emit("tensor",
                                lambda e, j=j, wbi=wbi, sp=sp, b=banks[j]:
                                e.matmul(pb[b][0:128, 0:T],
                                         wloB[wbi][:].bitcast(F32R)[:, j * 128:(j + 1) * 128],
                                         xhiB[wbi].bitcast(F32R),
                                         start=False, stop=sp),
                                inc="pe")
                        if ki == KC - 1 and bias_last is not None:
                            for j in range(4):
                                last_bank[banks[j]] = emit("tensor",
                                    lambda e, j=j, oc0=oc0, b=banks[j]:
                                    bias_last(e, oc0 + j, pb[b][0:128, 0:T]),
                                    waits=(list(bias_wait) if j == 0 else []),
                                    inc="pe")
                        wbuf_free[wri] = ("dv", lo_m)
                        whi_free[wbi] = ("pe", cnt["pe"])
                        xsp_free[wbi] = ("pe", cnt["pe"])
                        xlo_gate[0] = ("pe", cnt["pe"] - 4)
                    for j in range(4):
                        b = banks[j]
                        lb = last_bank[b]

                        def mk(oc=oc0 + j, b=b, lb=lb):
                            def do():
                                gw = list(getattr(evac, "guard", None) or ())
                                ev = emit("vector",
                                          lambda e, oc=oc, b=b:
                                          evac(e, oc, pb[b][0:128, 0:T]),
                                          waits=[("pe", lb)] + gw, inc="dv")
                                bank_free[b] = ("dv", ev)
                            return do
                        pending.append(mk())
                for fnp in pending:
                    fnp()

            def w_slice(dram_w):
                def f(kc, oc0, noc, nkc=1):
                    return dram_w[kc * 128:(kc + nkc) * 128,
                                  oc0 * 128:(oc0 + noc) * 128]
                return f

            def arena_release():
                pass

            bufA_guard = []
            bufB_guard = []
            bufC_guard = []
            for tile in range(n_tiles):
                t0 = tile * T
                av_guard = ([("st", cnt["st"]), ("su", cnt["su"])]
                            if tile else [])

                for kc in range(0, KC, 4):
                    emit("sync", lambda e, kc=kc:
                         e.dma_start(
                             out=xT[:, kc * T:(kc + 4) * T]
                             .rearrange("p (c t) -> p c t", c=4),
                             in_=d_xT[kc * 128:(kc + 4) * 128, t0:t0 + T]
                             .rearrange("(c p) t -> p c t", p=128)),
                         waits=([x_free[0]] if kc == 0 else ()), inc="lx", by=16)
                x_ld = cnt["lx"]

                def xrhs(kc):
                    return xT[:, kc * T:(kc + 1) * T]

                def mk_evac(dstbuf, guard=None):
                    first = [True]

                    def ev(e, oc, ps):
                        return e.tensor_copy(
                            dstbuf[:].rearrange("p (t h) -> p t h", h=NH)[:, :, oc],
                            ps)
                    ev.guard = guard
                    return ev

                emit("tensor", lambda e: e.nop(),
                     waits=[("lx", x_ld), ("ld", consts_ld)])
                xw = [("lx", x_ld)]
                stream_gemm(w_slice(d_wq), xrhs, mk_evac(bufA, bufA_guard),
                            rhs_wait=xw, w2ap=bufB[:, 0:1024],
                            w2wait=([("pe", cnt["pe"])] if tile else []))
                stream_gemm(w_slice(d_wk), xrhs, mk_evac(bufB, bufB_guard),
                            rhs_wait=xw, w2ap=avT[:, 0:1024],
                            w2wait=([("st", cnt["st"]), ("su", cnt["su"])]
                                    if tile else []))
                qk_dv = cnt["dv"]

                stream_gemm(w_slice(d_wv), xrhs, mk_evac(bufC, bufC_guard),
                            rhs_wait=xw, w2ap=avT[:, 0:1024])
                v_dv = cnt["dv"]

                emit("sync", lambda e: e.dma_start(out=mskBD[:], in_=d_msk[:]),
                     waits=[("pe", cnt["pe"]), ("dv", cnt["dv"])], inc="lm", by=16)
                msk_ld = cnt["lm"]

                pbS, pbB, pbV, pbA = (0, 1), (2, 3), (4, 5), (6, 7)
                NSG = T // 32
                score0_pe = [0]
                score1_pe = [0]
                exp_m = {}; nrm_m = {}; bdt_m = {}; vgt_m = {}
                bde_m = {}; vge_m = {}; avm_m = {}
                for s in range(NSG + 3):
                    if s < NSG:
                        g = s
                        i = g % 2
                        lastm = None
                        for j in range(4):
                            c0 = g * 512 + j * 128
                            w = []
                            if j == 0:
                                w = [("dv", qk_dv), bank_free[pbS[i]]]
                            lastm = emit("tensor", lambda e, c0=c0, j=j, i=i:
                                e.matmul(pb[pbS[i]][0:128, j * 128:(j + 1) * 128],
                                         bufA[:, c0:c0 + 128], bufB[:, c0:c0 + 128],
                                         start=(j == 0), stop=(j == 3)),
                                waits=w, inc="pe")
                        if g == 0:
                            score0_pe[0] = lastm
                        if g == min(1, NSG - 1):
                            score1_pe[0] = lastm
                        exp_m[g] = emit("scalar", lambda e, i=i:
                            e.activation(out=AN8[i][:], in_=pb[pbS[i]][0:128, 0:512],
                                         func=AF.Exp, scale=SCALE),
                            waits=[("pe", lastm)] +
                                  ([("pe", bdt_m[g - 2])] if g >= 2 else []),
                            inc="se")
                        bank_free[pbS[i]] = ("se", exp_m[g])
                    if 1 <= s <= NSG:
                        g = s - 1
                        i = g % 2
                        lastv = None
                        for j in range(4):
                            w = []
                            if j == 0:
                                w = [("dv", v_dv), bank_free[pbV[i]]]
                            lastv = emit("tensor", lambda e, g=g, j=j, i=i:
                                e.matmul(pb[pbV[i]][0:128, j * 128:(j + 1) * 128],
                                         bufC[:, g * 512 + j * 128:g * 512 + j * 128 + 128],
                                         i128[:], is_transpose=True,
                                         start=(j == 0), stop=(j == 3)),
                                waits=w, inc="pe")
                        vgt_m[g] = lastv
                        lastb = None
                        for j in range(4):
                            w = [nrm_m[g]] if j == 0 else []
                            if j == 0:
                                w.append(bank_free[pbB[i]])
                            lastb = emit("tensor", lambda e, j=j, i=i:
                                e.matmul(pb[pbB[i]][0:128, j * 128:(j + 1) * 128],
                                         AN8[i][:, j * 128:(j + 1) * 128],
                                         i128[:], is_transpose=True,
                                         start=(j == 0), stop=(j == 3)),
                                waits=w, inc="pe")
                        bdt_m[g] = lastb
                    if 2 <= s <= NSG + 1:
                        g = s - 2
                        i = g % 2
                        lasta = None
                        for j in range(4):
                            w = []
                            if j == 0:
                                w = [("se", bde_m[g]), ("se", vge_m[g]),
                                     bank_free[pbA[i]]]
                            lasta = emit("tensor", lambda e, j=j, i=i:
                                e.matmul(pb[pbA[i]][0:128, j * 128:(j + 1) * 128],
                                         VGs[i][:, j * 128:(j + 1) * 128],
                                         BDs[i][:, j * 128:(j + 1) * 128],
                                         start=(j == 0), stop=(j == 3)),
                                waits=w, inc="pe")
                        avm_m[g] = lasta
                    if 1 <= s <= NSG:
                        g = s - 1
                        i = g % 2
                        bde_m[g] = emit("scalar", lambda e, i=i:
                            e.copy(out=BDs[i][:], in_=pb[pbB[i]][0:128, 0:512]),
                            waits=[("pe", bdt_m[g])] +
                                  ([("pe", avm_m[g - 2])] if g >= 2 else []),
                            inc="se")
                        bank_free[pbB[i]] = ("se", bde_m[g])
                        vge_m[g] = emit("scalar", lambda e, i=i:
                            e.copy(out=VGs[i][:], in_=pb[pbV[i]][0:128, 0:512]),
                            waits=[("pe", vgt_m[g])] +
                                  ([("pe", avm_m[g - 2])] if g >= 2 else []),
                            inc="se")
                        bank_free[pbV[i]] = ("se", vge_m[g])
                    if 3 <= s <= NSG + 2:
                        g = s - 3
                        i = g % 2
                        ave = emit("vector", lambda e, g=g, i=i:
                            e.tensor_copy(avT[:].rearrange("p (h t) -> p h t", h=16)[:, :, g * 32:g * 32 + 32],
                                          pb[pbA[i]][0:128, 0:512]
                                          .rearrange("p (t h) -> p h t", h=16)),
                            waits=[("pe", avm_m[g])] + (av_guard if g == 0 else []),
                            inc="dv")
                        bank_free[pbA[i]] = ("dv", ave)
                    if s < NSG:
                        g = s
                        i = g % 2
                        for q4 in range(4):
                            emit("vector", lambda e, i=i, q4=q4:
                                 e.scalar_tensor_tensor(
                                     AN8[i][:, q4 * 128:(q4 + 1) * 128],
                                     AN8[i][:, q4 * 128:(q4 + 1) * 128],
                                     1.0, mskBD[:], op0=ALU.mult, op1=ALU.mult,
                                     accum_out=rsum8[i][:, q4:q4 + 1]),
                                 waits=([("se", exp_m[g]), ("lm", msk_ld)]
                                        if q4 == 0 else []))
                        emit("vector", lambda e: e.drain())
                        rcp = emit("vector", lambda e, i=i:
                                   e.reciprocal(rrec8[i][:], rsum8[i][:]), inc="dv")
                        emit("vector", lambda e: e.drain())
                        nrm_dv = emit("vector", lambda e, i=i:
                            e.tensor_tensor(AN8[i].rearrange("p (q t) -> p q t", q=4),
                                            AN8[i].rearrange("p (q t) -> p q t", q=4),
                                            rrec8[i][:].unsqueeze(2).broadcast_to((128, 4, 128)),
                                            op=ALU.mult), inc="dv")
                        nrm_m[g] = ("dv", nrm_dv)
                att_dv = cnt["dv"]
                bufA_guard = [("pe", cnt["pe"])]
                emit("gpsimd", lambda e: e.dma_start(
                     out=rwT[:].rearrange("p (kc e) -> p kc e", e=8),
                     in_=d_rw[:].rearrange("(kc p) e -> p kc e", p=128)),
                     waits=[("dv", att_dv), ("pe", cnt["pe"]), ("se", cnt["se"])],
                     inc="lg", by=16)
                emit("gpsimd", lambda e: e.dma_start(out=rb2s, in_=d_rb[:]),
                     inc="lg", by=16)
                rw_ld = cnt["lg"]

                def avrhs(kc):
                    return avT[:, kc * T:(kc + 1) * T]

                def wo_evac(e, oc, ps):
                    return e.tensor_tensor(xT[:, oc * T:(oc + 1) * T], ps,
                                           xT[:, oc * T:(oc + 1) * T],
                                           op=ALU.add)

                arena_release()
                emit("tensor", lambda e: e.nop(), waits=[("dv", att_dv)])
                stream_gemm(w_slice(d_wo), avrhs, wo_evac,
                            rhs_wait=[("dv", att_dv)],
                            w2ap=bufA[:, 0:1024],
                            w2wait=[("pe", score1_pe[0])])
                x1_dv = cnt["dv"]

                def gen_ln(src, dst, store_rows=None, src_ready=None,
                           fwd_banks=(0, 1), back_banks=(2, 3), sq_banks=(4, 5),
                           post_back=None, back_dv_out=None, bufs=None,
                           store_q="gpsimd"):
                    if bufs is None:
                        bufs = x1tok2
                    x1f = x1tok_free_g
                    q_dv = {}
                    nrm_of = {}
                    nrm_q_of = {}
                    lastT_b = {}
                    x1n_mark = [0]

                    def fwd(c):
                        bi = c % len(bufs)
                        si = c % 2
                        xt = bufs[bi]
                        lastT = None
                        for kc in range(KC):
                            bnk = fwd_banks[(c * 4 + kc // 4) % 2]
                            w = [bank_free[bnk]] if kc % 4 == 0 else []
                            if kc == 0 and src_ready is not None:
                                w.append(src_ready)
                            lastT = emit("tensor", lambda e, kc=kc, c=c, bnk=bnk:
                                e.matmul(pb[bnk][0:128, (kc % 4) * 128:(kc % 4) * 128 + 128],
                                         src[:, kc * T + c * 128: kc * T + c * 128 + 128],
                                         i128[:], is_transpose=True,
                                         start=(kc % 4 == 0), stop=(kc % 4 == 3)),
                                waits=w, inc="pe")
                            if kc % 4 == 3:
                                ev = emit("vector", lambda e, kc=kc, bnk=bnk, xt=xt, bi=bi:
                                    e.tensor_scalar(xt[:, (kc // 4) * 512:(kc // 4) * 512 + 512],
                                                    pb[bnk][0:128, 0:512], 1.0, 0.0,
                                                    op0=ALU.mult, op1=ALU.add,
                                                    accum_out=s1q[si][:, kc // 4:kc // 4 + 1]),
                                    waits=[("pe", lastT)] + ([x1f[bi]] if kc == 3 else []),
                                    inc="dv")
                                bank_free[bnk] = ("dv", ev)
                                q_dv[kc // 4] = ev
                        tok_dv = cnt["dv"]
                        emit("vector", lambda e: e.drain())
                        s1_dv = emit("vector", lambda e, si=si:
                             e.reduce_sum(s1[si][:], s1q[si][:], axis=AX.X),
                             inc="dv")
                        sqb = sq_banks[c % 2]
                        sqm = None
                        for q4 in range(4):
                            w = [("dv", q_dv[q4])]
                            if q4 == 0:
                                w.append(bank_free[sqb])
                            else:
                                emit("scalar", lambda e: e.drain())
                            sqm = emit("scalar", lambda e, q4=q4, xt=xt, si=si, sqb=sqb:
                                       e.activation(out=pb[sqb][0:128, 0:512],
                                                    in_=xt[:, q4 * 512:(q4 + 1) * 512],
                                                    func=AF.Square,
                                                    accum_out=s2q[si][:, q4:q4 + 1]),
                                       waits=w, inc="se")
                        bank_free[sqb] = ("se", sqm)
                        emit("vector", lambda e: e.drain())
                        s2_dv = emit("vector", lambda e, si=si:
                             e.reduce_sum(s2[si][:], s2q[si][:], axis=AX.X),
                             waits=[("se", sqm)], inc="dv")
                        emit("scalar", lambda e: e.drain())
                        mu_se = emit("scalar", lambda e, si=si:
                                     e.activation(out=muv[si][:], in_=s1[si][:],
                                                  func=AF.Copy, scale=1.0 / D),
                                     waits=[("dv", s1_dv)], inc="se")
                        emit("scalar", lambda e: e.drain())
                        msq_se = emit("scalar", lambda e, si=si:
                                      e.activation(out=msq[si][:], in_=muv[si][:],
                                                   func=AF.Square),
                                      inc="se")
                        emit("vector", lambda e: e.drain())
                        ve = emit("vector", lambda e, si=si:
                                  e.scalar_tensor_tensor(varv[si][:], s2[si][:], 1.0 / D,
                                                         msq[si][:], op0=ALU.mult,
                                                         op1=ALU.subtract),
                                  waits=[("se", msq_se)], inc="dv")
                        emit("scalar", lambda e: e.drain())
                        sde = emit("scalar", lambda e, si=si:
                                   e.activation(out=sdv[si][:], in_=varv[si][:],
                                                func=AF.Sqrt, scale=1.0, bias=epsv[:]),
                                   waits=[("dv", ve), ("dv", eps_dv)], inc="se")
                        emit("vector", lambda e, si=si: e.reciprocal(rstd[si][:], sdv[si][:]),
                             waits=[("se", sde)])
                        emit("vector", lambda e: e.drain())
                        nrm = emit("vector", lambda e, xt=xt, si=si:
                                   e.tensor_scalar(xt, xt, muv[si][:], rstd[si][:],
                                                   op0=ALU.subtract, op1=ALU.mult),
                                   inc="dv")
                        nrm_of[c] = nrm
                        nrm_q_of[c] = [nrm, nrm, nrm, nrm]
                        if store_rows is not None:
                            if store_q == "sync":
                                ssem = "sv"
                            else:
                                ssem = "st" if bi % 2 == 0 else "su"
                            emit(store_q, lambda e, c=c, xt=xt:
                                 e.dma_start(out=d_out[store_rows + c * 128:
                                                       store_rows + c * 128 + 128, :],
                                             in_=xt),
                                 waits=[("dv", nrm_of[c])], inc=ssem, by=16)
                            x1f[bi] = (ssem, cnt[ssem])

                    def back(c):
                        bi = c % len(bufs)
                        xt = bufs[bi]
                        lastT = None
                        for kc in range(KC):
                            bnk = back_banks[(c * 4 + kc // 4) % 2]
                            w = [("dv", nrm_q_of[c][kc // 4])] if kc % 4 == 0 else []
                            if kc % 4 == 0:
                                w.append(bank_free[bnk])
                            lastT = emit("tensor", lambda e, kc=kc, bnk=bnk, xt=xt:
                                e.matmul(pb[bnk][0:128, (kc % 4) * 128:(kc % 4) * 128 + 128],
                                         xt[:, kc * 128:(kc + 1) * 128],
                                         i128[:], is_transpose=True,
                                         start=(kc % 4 == 0), stop=(kc % 4 == 3)),
                                waits=w, inc="pe")
                            if kc % 4 == 3:
                                ev = emit("vector", lambda e, kc=kc, c=c, bnk=bnk:
                                    e.tensor_copy(
                                        dst[:].rearrange("p (kk t) -> p kk t", kk=KC)
                                        [:, kc - 3:kc + 1, c * 128:(c + 1) * 128],
                                        pb[bnk][0:128, 0:512]
                                        .rearrange("p (kk t) -> p kk t", kk=4)),
                                    waits=[("pe", lastT)], inc="dv")
                                bank_free[bnk] = ("dv", ev)
                        x1f[bi] = ("pe", lastT)
                        if back_dv_out is not None:
                            back_dv_out[c] = cnt["dv"]
                        x1n_mark[0] = cnt["dv"]

                    if store_rows is not None:
                        for c in range(NCH):
                            fwd(c)
                        for c in range(NCH):
                            bi = c % len(bufs)
                            ssem = ("sv" if store_q == "sync" else
                                    ("st" if bi % 2 == 0 else "su"))
                            x1f[bi] = (ssem, cnt[ssem])
                    else:
                        for s in range(NCH + 1):
                            if s < NCH:
                                fwd(s)
                            if s >= 1:
                                back(s - 1)
                                if post_back is not None:
                                    post_back(s - 1)
                    return x1n_mark[0]

                rbank = 6
                gbank = 7
                ge_m = {}; g2_m = {}; gxt_m = {}

                x1n_dv = gen_ln(xT, bufB, src_ready=("dv", x1_dv))
                x_free[0] = ("pe", cnt["pe"])

                emit("gpsimd", lambda e: e.dma_start(out=bb, in_=d_bb[:]),
                     waits=[("dv", x1n_dv), ("pe", cnt["pe"])], inc="lr", by=16)
                ffn_ld = cnt["lr"]
                emit("gpsimd", lambda e: e.dma_start(
                     out=laT.rearrange("p (kc r) -> p kc r", r=128),
                     in_=d_la[:].rearrange("(kc p) r -> p kc r", p=128)),
                     waits=[x1tok_free_g[1], ("dv", x1n_dv)],
                     inc="la", by=16)
                la_ld = cnt["la"]

                lastm = None
                rw_w = [("lg", rw_ld), ("dv", x1n_dv), bank_free[rbank]]
                for kc in range(KC):
                    lastm = emit("tensor", lambda e, kc=kc:
                        e.matmul(pb[rbank][0:8, 0:T],
                                 rwT[:, kc * 8:(kc + 1) * 8],
                                 bufB[:, kc * T:(kc + 1) * T],
                                 start=(kc == 0), stop=(kc == KC - 1)),
                        waits=(rw_w if kc == 0 else ()), inc="pe")
                lge = emit("vector", lambda e:
                           e.tensor_scalar(logT2[:, 0:T], pb[rbank][0:8, 0:T],
                                           rb2s, None, op0=ALU.add),
                           waits=[("pe", lastm), ("lg", rw_ld)], inc="dv")
                bank_free[rbank] = ("dv", lge)

                for c in range(NCH):
                    t = GT2[c]
                    lgt = emit("tensor", lambda e, c=c:
                        e.matmul(pb[gbank][0:128, 0:8],
                                 logT2[:, c * 128:(c + 1) * 128], i8[:],
                                 is_transpose=True, start=True, stop=True),
                        waits=[("dv", lge), bank_free[gbank]], inc="pe")
                    lev = emit("vector", lambda e, t=t:
                               e.tensor_copy(t["lgtok"], pb[gbank][0:128, 0:8]),
                               waits=[("pe", lgt)], inc="dv")
                    bank_free[gbank] = ("dv", lev)
                    emit("vector", lambda e: e.drain())
                    emit("vector", lambda e, t=t: e.reduce_max(t["m1"], t["lgtok"], axis=AX.X))
                    emit("vector", lambda e: e.drain())
                    emit("vector", lambda e, t=t: e.tensor_scalar(t["eq1"], t["lgtok"], t["m1"], None, op0=ALU.is_equal))
                    emit("vector", lambda e: e.drain())
                    emit("vector", lambda e, t=t: e.scalar_tensor_tensor(t["l2t"], t["eq1"], NEG_BIG, t["lgtok"], op0=ALU.mult, op1=ALU.add))
                    emit("vector", lambda e: e.drain())
                    emit("vector", lambda e, t=t: e.reduce_max(t["m2"], t["l2t"], axis=AX.X))
                    emit("vector", lambda e: e.drain())
                    g1 = emit("vector", lambda e, t=t: e.tensor_tensor(t["ddt"], t["m2"], t["m1"], op=ALU.subtract), inc="dv")
                    emit("vector", lambda e, t=t: e.tensor_scalar(t["eq2"], t["lgtok"], t["m2"], None, op0=ALU.is_equal))
                    ge_m[c] = emit("scalar", lambda e, t=t:
                                   e.activation(out=t["e2t"], in_=t["ddt"], func=AF.Exp),
                                   waits=[("dv", g1)], inc="se")

                for c in range(NCH):
                    t = GT2[c]
                    emit("vector", lambda e, t=t: e.tensor_scalar(t["dent"], t["e2t"], 1.0, None, op0=ALU.add),
                         waits=[("se", ge_m[c])])
                    emit("vector", lambda e: e.drain())
                    emit("vector", lambda e, t=t: e.reciprocal(t["rden"], t["dent"]))
                    emit("vector", lambda e: e.drain())
                    emit("vector", lambda e, t=t: e.tensor_scalar(t["w2s"], t["e2t"], t["rden"], 2.0, op0=ALU.mult, op1=ALU.mult))
                    emit("vector", lambda e, t=t: e.tensor_scalar(t["gatev"], t["eq1"], t["rden"], 2.0, op0=ALU.mult, op1=ALU.mult))
                    emit("vector", lambda e: e.drain())
                    emit("vector", lambda e, t=t: e.scalar_tensor_tensor(t["gatev"], t["eq2"], t["w2s"], t["gatev"], op0=ALU.mult, op1=ALU.add))
                    emit("vector", lambda e: e.drain())
                    g2_m[c] = emit("vector", lambda e, t=t, c=c:
                              e.tensor_copy(gexp2b[c].rearrange("p (e r) -> p e r", r=16),
                                            t["gatev"].unsqueeze(2).broadcast_to((128, 8, 16))),
                              inc="dv")

                labank = rbank
                lastm = None
                la_w = [("la", la_ld), ("dv", x1n_dv), bank_free[labank]]
                for kc in range(KC):
                    lastm = emit("tensor", lambda e, kc=kc:
                        e.matmul(pb[labank][0:128, 0:T],
                                 laT[:, kc * 128:(kc + 1) * 128],
                                 bufB[:, kc * T:(kc + 1) * T],
                                 start=(kc == 0), stop=(kc == KC - 1)),
                        waits=(la_w if kc == 0 else ()), inc="pe")
                la_pe = lastm

                for c in range(NCH):
                    w = [("dv", g2_m[c])]
                    if c == 0:
                        w.append(bank_free[gbank])
                    gxt_m[c] = emit("tensor", lambda e, c=c:
                        e.matmul(pb[gbank][0:128, c * 128:(c + 1) * 128],
                                 gexp2b[c], i128[:],
                                 is_transpose=True, start=(c == 0),
                                 stop=(c == NCH - 1)),
                        waits=w, inc="pe")

                bias_wait_l = [("lb", 0)]

                def base_mid():
                    emit("vector", lambda e:
                        e.tensor_copy(gateT2[:, 0:T], pb[gbank][0:128, 0:T]),
                        waits=[("pe", gxt_m[NCH - 1])], inc="dv")
                    gte_dv = cnt["dv"]
                    emit("vector", lambda e: e.drain())
                    gme = emit("vector", lambda e:
                        e.tensor_tensor(gmidT2[:, 0:T], pb[labank][0:128, 0:T],
                                        gateT2[:, 0:T], op=ALU.mult),
                        waits=[("pe", la_pe), ("dv", gte_dv)], inc="dv")
                    bank_free[labank] = ("dv", gme)
                    bank_free[gbank] = ("dv", gme)
                    bias_wait_l.append(("dv", gme))

                def x1nrhs(kc):
                    return bufB[:, kc * T:(kc + 1) * T]

                def lb_last(e, oc, ps_ap):
                    return e.matmul(ps_ap, lbF[:, oc * 128:(oc + 1) * 128],
                                    gmidT2[:, 0:T], start=False, stop=True)

                def base_evac(e, oc, ps):
                    return e.scalar_tensor_tensor(bufC[:, oc * T:(oc + 1) * T],
                                                  ps, bb[:, oc:oc + 1],
                                                  bufB[:, oc * T:(oc + 1) * T],
                                                  op0=ALU.add, op1=ALU.add)

                emit("gpsimd", lambda e: e.dma_start(out=lbF, in_=d_lb[:]),
                     waits=[("pe", cnt["pe"])], inc="lb", by=16)
                lb_ld = cnt["lb"]
                bias_wait_l[0] = ("lb", lb_ld)
                base_evac.guard = [("lr", ffn_ld)]
                stream_gemm(w_slice(d_bw), x1nrhs, base_evac,
                            rhs_wait=[("dv", x1n_dv)], bias_last=lb_last,
                            bias_wait=bias_wait_l, mid_cb=base_mid,
                            w2ap=bufA[:, 0:1024])
                x2_dv = cnt["dv"]
                bufB_guard = [("se", cnt["se"]), ("dv", cnt["dv"])]

                ln2_bufs = [x1tok2[0], x1tok2[1]]
                if NCH > 2:
                    ln2_bufs += [avT[:, 0:2048], avT[:, 2048:4096]]
                gen_ln(bufC, None, store_rows=t0, src_ready=("dv", x2_dv),
                       fwd_banks=(4, 5), sq_banks=(6, 7), bufs=ln2_bufs,
                       store_q=("sync" if tile == n_tiles - 1 else "gpsimd"))
                bufC_guard = [("pe", cnt["pe"]), ("se", cnt["se"])]

            emit("gpsimd", lambda e: e.nop(),
                 waits=[("st", cnt["st"]), ("su", cnt["su"]),
                        ("sv", cnt["sv"])])

        prev_av = [None, None]

        def run_engine(eng_name, E):
            prev_av[0] = None
            prev_av[1] = None
            program(eng_name, E)

        @block.sync
        def _(sync):
            run_engine("sync", sync)

        @block.tensor
        def _(tensor):
            run_engine("tensor", tensor)

        @block.vector
        def _(vector):
            run_engine("vector", vector)

        @block.scalar
        def _(scalar):
            run_engine("scalar", scalar)

        @block.gpsimd
        def _(gpsimd):
            run_engine("gpsimd", gpsimd)

    return nc


def _prep_inputs(L0_attn_x, L0_attn_Wq, L0_attn_Wk, L0_attn_Wv, L0_attn_Wo,
                 L0_ffn_base_w, L0_ffn_base_b, L0_ffn_router_w, L0_ffn_router_b,
                 L0_ffn_lora_A, L0_ffn_lora_B, n_cores, shard_t):
    x2 = np.ascontiguousarray(np.asarray(L0_attn_x, dtype=np.float32)[0])
    xT = np.ascontiguousarray(x2.T)
    base = {
        "wqT": np.ascontiguousarray(np.asarray(L0_attn_Wq, np.float32).T),
        "wkT": np.ascontiguousarray(np.asarray(L0_attn_Wk, np.float32).T),
        "wvT": np.ascontiguousarray(np.asarray(L0_attn_Wv, np.float32).T),
        "woT": np.ascontiguousarray(np.asarray(L0_attn_Wo, np.float32).T),
        "bwT": np.ascontiguousarray(np.asarray(L0_ffn_base_w, np.float32).T),
        "laT": np.ascontiguousarray(
            np.asarray(L0_ffn_lora_A, np.float32).transpose(1, 0, 2).reshape(D, 128)),
        "lbF": np.ascontiguousarray(np.asarray(L0_ffn_lora_B, np.float32).reshape(128, D)),
        "rwT": np.ascontiguousarray(np.asarray(L0_ffn_router_w, np.float32).T),
        "bb2": np.ascontiguousarray(np.asarray(L0_ffn_base_b, np.float32).reshape(KC, 128).T),
        "rb2": np.ascontiguousarray(np.asarray(L0_ffn_router_b, np.float32).reshape(8, 1)),
        "id128": np.eye(128, dtype=np.float32),
        "id16": np.eye(16, dtype=np.float32),
        "id8": np.eye(8, dtype=np.float32),
        "maskbd": np.kron(np.eye(8, dtype=np.float32),
                          np.ones((16, 16), dtype=np.float32)),
    }
    in_maps = []
    for c in range(n_cores):
        m = dict(base)
        m["xT"] = np.ascontiguousarray(xT[:, c * shard_t:(c + 1) * shard_t])
        in_maps.append(m)
    return in_maps


def kernel(**inputs):
    n_cores = 8
    S = inputs["L0_attn_x"].shape[1]
    shard_t = S // n_cores
    nc = build_nc(shard_t=shard_t, tile_t=512)
    in_maps = _prep_inputs(n_cores=n_cores, shard_t=shard_t, **inputs)
    res = run_bass_kernel_spmd(nc, in_maps, core_ids=list(range(n_cores)))
    out = np.concatenate([r["out"] for r in res.results], axis=0)
    return out.reshape(1, S, D)


# version 37
# speedup vs baseline: 1.0095x; 1.0095x over previous
import math
import contextlib
import numpy as np
import concourse.bass as bass
import concourse.mybir as mybir
from concourse.bass_utils import run_bass_kernel_spmd

F32 = mybir.dt.float32
F32R = mybir.dt.float32r
AF = mybir.ActivationFunctionType
ALU = mybir.AluOpType
AX = mybir.AxisListType

D = 2048
KC = D // 128
NH = 16
SCALE = 1.0 / math.sqrt(128.0)
NEG_BIG = -1e30
EPS = 1e-5

ENGINES = ("sync", "tensor", "vector", "scalar", "gpsimd")


def build_nc(shard_t, tile_t):
    n_tiles = shard_t // tile_t
    T = tile_t
    NCH = T // 128

    nc = bass.Bass("TRN2", target_bir_lowering=False)

    d_xT = nc.dram_tensor("xT", [D, shard_t], F32, kind="ExternalInput").ap()
    d_wq = nc.dram_tensor("wqT", [D, D], F32, kind="ExternalInput").ap()
    d_wk = nc.dram_tensor("wkT", [D, D], F32, kind="ExternalInput").ap()
    d_wv = nc.dram_tensor("wvT", [D, D], F32, kind="ExternalInput").ap()
    d_wo = nc.dram_tensor("woT", [D, D], F32, kind="ExternalInput").ap()
    d_bw = nc.dram_tensor("bwT", [D, D], F32, kind="ExternalInput").ap()
    d_la = nc.dram_tensor("laT", [D, 128], F32, kind="ExternalInput").ap()
    d_lb = nc.dram_tensor("lbF", [128, D], F32, kind="ExternalInput").ap()
    d_rw = nc.dram_tensor("rwT", [D, 8], F32, kind="ExternalInput").ap()
    d_bb = nc.dram_tensor("bb2", [128, KC], F32, kind="ExternalInput").ap()
    d_rb = nc.dram_tensor("rb2", [8, 1], F32, kind="ExternalInput").ap()
    d_i128 = nc.dram_tensor("id128", [128, 128], F32, kind="ExternalInput").ap()
    d_i16 = nc.dram_tensor("id16", [16, 16], F32, kind="ExternalInput").ap()
    d_i8 = nc.dram_tensor("id8", [8, 8], F32, kind="ExternalInput").ap()
    d_msk = nc.dram_tensor("maskbd", [128, 128], F32, kind="ExternalInput").ap()
    d_out = nc.dram_tensor("out", [shard_t, D], F32, kind="ExternalOutput").ap()

    with contextlib.ExitStack() as ctx:
        def sbuf(name, shape, dt=F32):
            return ctx.enter_context(nc.sbuf_tensor(name, shape, dt))

        xT = sbuf("xTs", [128, KC * T])
        bufA = sbuf("bufA", [128, NH * T])
        bufB = sbuf("bufB", [128, KC * T])
        bufC = sbuf("bufC", [128, KC * T])
        avT = sbuf("avT", [128, NH * T])
        SH = sbuf("SH", [128, 4096])
        ARENA = sbuf("ARENA", [128, 3072])
        xsp = sbuf("xsp", [128, 3 * T])
        wrawD = sbuf("wrawD", [128, 1024])
        whiB = [sbuf("whi0", [128, 512]), sbuf("whi1", [128, 512])]
        wloB = [sbuf("wlo0", [128, 512]), sbuf("wlo1", [128, 512])]
        i128 = sbuf("i128s", [128, 128])
        i8 = sbuf("i8s", [8, 8])
        AN8 = [ARENA[:, 0:512], ARENA[:, 512:1024]]
        BDs = [ARENA[:, 1024:1536], ARENA[:, 1536:2048]]
        VGs = [ARENA[:, 2048:2560], ARENA[:, 2560:3072]]
        logT2 = ARENA[0:8, 0:512]
        gexp2b = [ARENA[:, 2048 + 128 * _c:2048 + 128 * (_c + 1)]
                  for _c in range(4)]
        rb2s = ARENA[0:8, 896:897]
        gateT2 = ARENA[:, 1024:1536]
        gmidT2 = ARENA[:, 1536:2048]
        GT2 = []
        for _par in range(4):
            _p = 640 + 64 * _par
            GT2.append(dict(
                lgtok=ARENA[:, _p:_p + 8], m1=ARENA[:, _p + 8:_p + 9],
                eq1=ARENA[:, _p + 16:_p + 24], l2t=ARENA[:, _p + 24:_p + 32],
                m2=ARENA[:, _p + 32:_p + 33], ddt=ARENA[:, _p + 33:_p + 34],
                e2t=ARENA[:, _p + 34:_p + 35], dent=ARENA[:, _p + 35:_p + 36],
                rden=ARENA[:, _p + 36:_p + 37], w2s=ARENA[:, _p + 37:_p + 38],
                eq2=ARENA[:, _p + 40:_p + 48], gatev=ARENA[:, _p + 48:_p + 56]))
        xhiB = [xsp[:, 0:T], xsp[:, T:2 * T]]
        xloB = [xsp[:, 2 * T:3 * T], xsp[:, 2 * T:3 * T]]
        cb128 = sbuf("cb128", [128, 128])
        mskBD = cb128
        rsum8 = [sbuf("rsum8a", [128, 4]), sbuf("rsum8b", [128, 4])]
        rrec8 = [sbuf("rrec8a", [128, 4]), sbuf("rrec8b", [128, 4])]
        s1 = [sbuf("s1a", [128, 1]), sbuf("s1b", [128, 1])]
        s2 = [sbuf("s2a", [128, 1]), sbuf("s2b", [128, 1])]
        muv = [sbuf("muva", [128, 1]), sbuf("muvb", [128, 1])]
        msq = [sbuf("msqa", [128, 1]), sbuf("msqb", [128, 1])]
        varv = [sbuf("varva", [128, 1]), sbuf("varvb", [128, 1])]
        sdv = [sbuf("sdva", [128, 1]), sbuf("sdvb", [128, 1])]
        rstd = [sbuf("rstda", [128, 1]), sbuf("rstdb", [128, 1])]

        x1tok2 = [SH[:, 0:2048], SH[:, 2048:4096]]
        gateT = SH[:, 0:T]
        gmidT = SH[:, T:2 * T]
        logT = SH[0:8, 2 * T:3 * T]
        gexp = SH[:, 3 * T:3 * T + 128]
        rwT = cb128
        GT = []
        for _par in range(2):
            _p = 3 * T + 128 + 64 * _par
            GT.append(dict(
                lgtok=SH[:, _p:_p + 8], m1=SH[:, _p + 8:_p + 9],
                eq1=SH[:, _p + 16:_p + 24], l2t=SH[:, _p + 24:_p + 32],
                m2=SH[:, _p + 32:_p + 33], ddt=SH[:, _p + 33:_p + 34],
                e2t=SH[:, _p + 34:_p + 35], dent=SH[:, _p + 35:_p + 36],
                rden=SH[:, _p + 36:_p + 37], w2s=SH[:, _p + 37:_p + 38],
                eq2=SH[:, _p + 40:_p + 48], gatev=SH[:, _p + 48:_p + 56]))
        bb = SH[:, 3 * T + 256:3 * T + 272]
        rb = SH[0:8, 3 * T + 272:3 * T + 273]
        laT = SH[:, 2048:4096]
        lbF = SH[:, 2048:4096]
        s2q = [sbuf("s2qa", [128, 4]), sbuf("s2qb", [128, 4])]
        s1q = [sbuf("s1qa", [128, 4]), sbuf("s1qb", [128, 4])]
        epsv = sbuf("epsv", [128, 1])
        halfv = sbuf("halfv", [128, 1])

        pb = [ctx.enter_context(nc.psum_tensor(f"pb{i}", [128, 512], F32))
              for i in range(8)]

        sem = {n: ctx.enter_context(nc.semaphore(n))
               for n in ("ld", "lx", "lm", "lr", "lg", "lb", "la", "wa", "wb",
                         "wc", "pe", "dv", "se", "st", "su")}
        block = ctx.enter_context(nc.Block())

        def program(eng_name, E):
            cnt = {n: 0 for n in sem}
            bank_free = [("dv", 0)] * 8
            wbuf_sem = {0: "wa", 1: "wb"}
            wbuf_free = {0: ("pe", 0), 1: ("pe", 0), 2: ("pe", 0), 3: ("pe", 0)}
            wcount = [0]
            pending_g = []

            def flush_pending():
                for fnp in pending_g:
                    fnp()
                del pending_g[:]

            def emit(engine, fn, waits=(), inc=None, by=1):
                if inc is not None:
                    cnt[inc] += by
                if engine == eng_name and E is not None:
                    for s, v in waits:
                        if v > 0:
                            E.wait_ge(sem[s], v)
                    r = fn(E)
                    if inc is not None:
                        r.then_inc(sem[inc], by)
                return cnt[inc] if inc is not None else None

            def barrier_waits():
                return [(s, cnt[s]) for s in ("pe", "dv", "se", "ld", "st")]

            def barrier():
                return barrier_waits()

            for dst, src in [(i128, d_i128), (i8, d_i8)]:
                emit("sync", lambda e, dst=dst, src=src:
                     e.dma_start(out=dst[:], in_=src[:]), inc="ld", by=16)
            consts_ld = cnt["ld"]
            emit("vector", lambda e: e.memset(epsv[:], EPS), inc="dv")
            emit("vector", lambda e: e.memset(halfv[:], 0.5), inc="dv")
            eps_dv = cnt["dv"]


            whi_free = {0: ("pe", 0), 1: ("pe", 0)}
            xsp_free = {0: ("pe", 0), 1: ("pe", 0)}
            xlo_gate = [("pe", 0)]
            x1tok_free_g = [("dv", 0), ("dv", 0), ("dv", 0), ("dv", 0)]
            x_free = [("dv", 0)]

            def stream_gemm(wsrc, rhs_of_kc, evac, rhs_wait=(), bias_first=None,
                            bias_last=None, bias_wait=(), mid_cb=None,
                            w2ap=None, w2wait=()):
                pending = pending_g
                s2_used = [False]
                for q in range(4):
                    oc0 = q * 4
                    banks = [(q % 2) * 4 + j for j in range(4)]
                    last_bank = {}
                    for ki in range(KC):
                        kc = ki
                        wbi = (q * KC + ki) % 2
                        wri = wcount[0] % 4
                        wcount[0] += 1
                        pair = wrawD if wri < 2 else w2ap
                        wsem = wbuf_sem[wri // 2]
                        if wri % 2 == 0:
                            dw = [wbuf_free[wri], wbuf_free[wri + 1]]
                            if wri == 2 and not s2_used[0]:
                                dw += list(w2wait)
                                s2_used[0] = True
                            emit("sync", lambda e, kc=kc, oc0=oc0, pair=pair:
                                 e.dma_start(
                                     out=pair.rearrange("p (c e) -> p c e", c=2),
                                     in_=wsrc(kc, oc0, 4, nkc=2)
                                     .rearrange("(c p) e -> p c e", p=128)),
                                 waits=dw, inc=wsem, by=16)
                        wready = cnt[wsem]
                        wr_ap = pair[:, (wri % 2) * 512:(wri % 2) * 512 + 512]
                        hi_m = emit("scalar", lambda e, wbi=wbi, wr_ap=wr_ap:
                            e.copy(out=whiB[wbi][:].bitcast(F32R), in_=wr_ap),
                            waits=[(wsem, wready), whi_free[wbi]], inc="se")
                        lo_m = emit("vector", lambda e, wbi=wbi, wr_ap=wr_ap:
                            e.tensor_sub(wloB[wbi][:].bitcast(F32R), wr_ap,
                                         whiB[wbi][:].bitcast(F32R)),
                            waits=[(wsem, wready), ("se", hi_m), whi_free[wbi]],
                            inc="dv")
                        xw = list(rhs_wait) if (q == 0 and ki == 0) else []
                        xhi_m = emit("scalar", lambda e, kc=kc, wbi=wbi:
                            e.copy(out=xhiB[wbi].bitcast(F32R), in_=rhs_of_kc(kc)),
                            waits=xw + [xsp_free[wbi]], inc="se")
                        xlo_m = emit("vector", lambda e, kc=kc, wbi=wbi:
                            e.tensor_sub(xloB[wbi].bitcast(F32R), rhs_of_kc(kc),
                                         xhiB[wbi].bitcast(F32R)),
                            waits=[("se", xhi_m), xsp_free[wbi], xlo_gate[0]],
                            inc="dv")
                        if q == 0 and ki == 2 and mid_cb is not None:
                            mid_cb()
                        if ki == 2 and pending:
                            for fnp in pending:
                                fnp()
                            del pending[:]
                        w0 = [("se", hi_m), ("se", xhi_m),
                              ("dv", lo_m), ("dv", xlo_m)]
                        if ki == 0:
                            for j in range(4):
                                w0.append(bank_free[banks[j]])
                            if bias_first is not None:
                                for j in range(4):
                                    emit("tensor", lambda e, j=j, oc0=oc0, b=banks[j]:
                                         bias_first(e, oc0 + j, pb[b][0:128, 0:T]),
                                         waits=(list(w0) if j == 0 else []), inc="pe")
                        st = (ki == 0 and bias_first is None)
                        for j in range(4):
                            emit("tensor", lambda e, j=j, wbi=wbi, st=st, b=banks[j]:
                                e.matmul(pb[b][0:128, 0:T],
                                         whiB[wbi][:].bitcast(F32R)[:, j * 128:(j + 1) * 128],
                                         xhiB[wbi].bitcast(F32R),
                                         start=st, stop=False),
                                waits=(w0 if j == 0 else []), inc="pe")
                        for j in range(4):
                            emit("tensor", lambda e, j=j, wbi=wbi, b=banks[j]:
                                e.matmul(pb[b][0:128, 0:T],
                                         whiB[wbi][:].bitcast(F32R)[:, j * 128:(j + 1) * 128],
                                         xloB[wbi].bitcast(F32R),
                                         start=False, stop=False),
                                inc="pe")
                        for j in range(4):
                            sp = (ki == KC - 1) and (bias_last is None)
                            last_bank[banks[j]] = emit("tensor",
                                lambda e, j=j, wbi=wbi, sp=sp, b=banks[j]:
                                e.matmul(pb[b][0:128, 0:T],
                                         wloB[wbi][:].bitcast(F32R)[:, j * 128:(j + 1) * 128],
                                         xhiB[wbi].bitcast(F32R),
                                         start=False, stop=sp),
                                inc="pe")
                        if ki == KC - 1 and bias_last is not None:
                            for j in range(4):
                                last_bank[banks[j]] = emit("tensor",
                                    lambda e, j=j, oc0=oc0, b=banks[j]:
                                    bias_last(e, oc0 + j, pb[b][0:128, 0:T]),
                                    waits=(list(bias_wait) if j == 0 else []),
                                    inc="pe")
                        wbuf_free[wri] = ("dv", lo_m)
                        whi_free[wbi] = ("pe", cnt["pe"])
                        xsp_free[wbi] = ("pe", cnt["pe"])
                        xlo_gate[0] = ("pe", cnt["pe"] - 4)
                    for j in range(4):
                        b = banks[j]
                        lb = last_bank[b]

                        def mk(oc=oc0 + j, b=b, lb=lb):
                            def do():
                                gw = list(getattr(evac, "guard", None) or ())
                                ev = emit("vector",
                                          lambda e, oc=oc, b=b:
                                          evac(e, oc, pb[b][0:128, 0:T]),
                                          waits=[("pe", lb)] + gw, inc="dv")
                                bank_free[b] = ("dv", ev)
                            return do
                        pending.append(mk())

            def w_slice(dram_w):
                def f(kc, oc0, noc, nkc=1):
                    return dram_w[kc * 128:(kc + nkc) * 128,
                                  oc0 * 128:(oc0 + noc) * 128]
                return f

            def arena_release():
                pass

            bufA_guard = []
            bufB_guard = []
            bufC_guard = []
            for tile in range(n_tiles):
                t0 = tile * T
                av_guard = ([("st", cnt["st"]), ("su", cnt["su"])]
                            if tile else [])

                for kc in range(0, KC, 4):
                    emit("sync", lambda e, kc=kc:
                         e.dma_start(
                             out=xT[:, kc * T:(kc + 4) * T]
                             .rearrange("p (c t) -> p c t", c=4),
                             in_=d_xT[kc * 128:(kc + 4) * 128, t0:t0 + T]
                             .rearrange("(c p) t -> p c t", p=128)),
                         waits=([x_free[0]] if kc == 0 else ()), inc="lx", by=16)
                x_ld = cnt["lx"]

                def xrhs(kc):
                    return xT[:, kc * T:(kc + 1) * T]

                def mk_evac(dstbuf, guard=None):
                    first = [True]

                    def ev(e, oc, ps):
                        return e.tensor_copy(
                            dstbuf[:].rearrange("p (t h) -> p t h", h=NH)[:, :, oc],
                            ps)
                    ev.guard = guard
                    return ev

                emit("tensor", lambda e: e.nop(),
                     waits=[("lx", x_ld), ("ld", consts_ld)])
                xw = [("lx", x_ld)]
                stream_gemm(w_slice(d_wq), xrhs, mk_evac(bufA, bufA_guard),
                            rhs_wait=xw, w2ap=bufB[:, 0:1024],
                            w2wait=([("pe", cnt["pe"])] if tile else []))
                stream_gemm(w_slice(d_wk), xrhs, mk_evac(bufB, bufB_guard),
                            rhs_wait=xw, w2ap=avT[:, 0:1024],
                            w2wait=([("st", cnt["st"]), ("su", cnt["su"])]
                                    if tile else []))

                stream_gemm(w_slice(d_wv), xrhs, mk_evac(bufC, bufC_guard),
                            rhs_wait=xw, w2ap=avT[:, 0:1024])
                flush_pending()
                qk_dv = cnt["dv"]
                v_dv = cnt["dv"]

                emit("sync", lambda e: e.dma_start(out=mskBD[:], in_=d_msk[:]),
                     waits=[("pe", cnt["pe"]), ("dv", cnt["dv"])], inc="lm", by=16)
                msk_ld = cnt["lm"]

                pbS, pbB, pbV, pbA = (0, 1), (2, 3), (4, 5), (6, 7)
                NSG = T // 32
                score0_pe = [0]
                score1_pe = [0]
                exp_m = {}; nrm_m = {}; bdt_m = {}; vgt_m = {}
                bde_m = {}; vge_m = {}; avm_m = {}
                for s in range(NSG + 3):
                    if s < NSG:
                        g = s
                        i = g % 2
                        lastm = None
                        for j in range(4):
                            c0 = g * 512 + j * 128
                            w = []
                            if j == 0:
                                w = [("dv", qk_dv), bank_free[pbS[i]]]
                            lastm = emit("tensor", lambda e, c0=c0, j=j, i=i:
                                e.matmul(pb[pbS[i]][0:128, j * 128:(j + 1) * 128],
                                         bufA[:, c0:c0 + 128], bufB[:, c0:c0 + 128],
                                         start=(j == 0), stop=(j == 3)),
                                waits=w, inc="pe")
                        if g == 0:
                            score0_pe[0] = lastm
                        if g == min(1, NSG - 1):
                            score1_pe[0] = lastm
                        exp_m[g] = emit("scalar", lambda e, i=i:
                            e.activation(out=AN8[i][:], in_=pb[pbS[i]][0:128, 0:512],
                                         func=AF.Exp, scale=SCALE),
                            waits=[("pe", lastm)] +
                                  ([("pe", bdt_m[g - 2])] if g >= 2 else []),
                            inc="se")
                        bank_free[pbS[i]] = ("se", exp_m[g])
                    if 1 <= s <= NSG:
                        g = s - 1
                        i = g % 2
                        lastv = None
                        for j in range(4):
                            w = []
                            if j == 0:
                                w = [("dv", v_dv), bank_free[pbV[i]]]
                            lastv = emit("tensor", lambda e, g=g, j=j, i=i:
                                e.matmul(pb[pbV[i]][0:128, j * 128:(j + 1) * 128],
                                         bufC[:, g * 512 + j * 128:g * 512 + j * 128 + 128],
                                         i128[:], is_transpose=True,
                                         start=(j == 0), stop=(j == 3)),
                                waits=w, inc="pe")
                        vgt_m[g] = lastv
                        lastb = None
                        for j in range(4):
                            w = [nrm_m[g]] if j == 0 else []
                            if j == 0:
                                w.append(bank_free[pbB[i]])
                            lastb = emit("tensor", lambda e, j=j, i=i:
                                e.matmul(pb[pbB[i]][0:128, j * 128:(j + 1) * 128],
                                         AN8[i][:, j * 128:(j + 1) * 128],
                                         i128[:], is_transpose=True,
                                         start=(j == 0), stop=(j == 3)),
                                waits=w, inc="pe")
                        bdt_m[g] = lastb
                    if 2 <= s <= NSG + 1:
                        g = s - 2
                        i = g % 2
                        lasta = None
                        for j in range(4):
                            w = []
                            if j == 0:
                                w = [("se", bde_m[g]), ("se", vge_m[g]),
                                     bank_free[pbA[i]]]
                            lasta = emit("tensor", lambda e, j=j, i=i:
                                e.matmul(pb[pbA[i]][0:128, j * 128:(j + 1) * 128],
                                         VGs[i][:, j * 128:(j + 1) * 128],
                                         BDs[i][:, j * 128:(j + 1) * 128],
                                         start=(j == 0), stop=(j == 3)),
                                waits=w, inc="pe")
                        avm_m[g] = lasta
                    if 1 <= s <= NSG:
                        g = s - 1
                        i = g % 2
                        bde_m[g] = emit("scalar", lambda e, i=i:
                            e.copy(out=BDs[i][:], in_=pb[pbB[i]][0:128, 0:512]),
                            waits=[("pe", bdt_m[g])] +
                                  ([("pe", avm_m[g - 2])] if g >= 2 else []),
                            inc="se")
                        bank_free[pbB[i]] = ("se", bde_m[g])
                        vge_m[g] = emit("scalar", lambda e, i=i:
                            e.copy(out=VGs[i][:], in_=pb[pbV[i]][0:128, 0:512]),
                            waits=[("pe", vgt_m[g])] +
                                  ([("pe", avm_m[g - 2])] if g >= 2 else []),
                            inc="se")
                        bank_free[pbV[i]] = ("se", vge_m[g])
                    if 3 <= s <= NSG + 2:
                        g = s - 3
                        i = g % 2
                        ave = emit("vector", lambda e, g=g, i=i:
                            e.tensor_copy(avT[:].rearrange("p (h t) -> p h t", h=16)[:, :, g * 32:g * 32 + 32],
                                          pb[pbA[i]][0:128, 0:512]
                                          .rearrange("p (t h) -> p h t", h=16)),
                            waits=[("pe", avm_m[g])] + (av_guard if g == 0 else []),
                            inc="dv")
                        bank_free[pbA[i]] = ("dv", ave)
                    if s < NSG:
                        g = s
                        i = g % 2
                        for q4 in range(4):
                            emit("vector", lambda e, i=i, q4=q4:
                                 e.scalar_tensor_tensor(
                                     AN8[i][:, q4 * 128:(q4 + 1) * 128],
                                     AN8[i][:, q4 * 128:(q4 + 1) * 128],
                                     1.0, mskBD[:], op0=ALU.mult, op1=ALU.mult,
                                     accum_out=rsum8[i][:, q4:q4 + 1]),
                                 waits=([("se", exp_m[g]), ("lm", msk_ld)]
                                        if q4 == 0 else []))
                        emit("vector", lambda e: e.drain())
                        rcp = emit("vector", lambda e, i=i:
                                   e.reciprocal(rrec8[i][:], rsum8[i][:]), inc="dv")
                        emit("vector", lambda e: e.drain())
                        nrm_dv = emit("vector", lambda e, i=i:
                            e.tensor_tensor(AN8[i].rearrange("p (q t) -> p q t", q=4),
                                            AN8[i].rearrange("p (q t) -> p q t", q=4),
                                            rrec8[i][:].unsqueeze(2).broadcast_to((128, 4, 128)),
                                            op=ALU.mult), inc="dv")
                        nrm_m[g] = ("dv", nrm_dv)
                att_dv = cnt["dv"]
                bufA_guard = [("pe", cnt["pe"])]
                emit("gpsimd", lambda e: e.dma_start(
                     out=rwT[:].rearrange("p (kc e) -> p kc e", e=8),
                     in_=d_rw[:].rearrange("(kc p) e -> p kc e", p=128)),
                     waits=[("dv", att_dv), ("pe", cnt["pe"]), ("se", cnt["se"])],
                     inc="lg", by=16)
                emit("gpsimd", lambda e: e.dma_start(out=rb2s, in_=d_rb[:]),
                     inc="lg", by=16)
                rw_ld = cnt["lg"]

                def avrhs(kc):
                    return avT[:, kc * T:(kc + 1) * T]

                def wo_evac(e, oc, ps):
                    return e.tensor_tensor(xT[:, oc * T:(oc + 1) * T], ps,
                                           xT[:, oc * T:(oc + 1) * T],
                                           op=ALU.add)

                arena_release()
                emit("tensor", lambda e: e.nop(), waits=[("dv", att_dv)])
                stream_gemm(w_slice(d_wo), avrhs, wo_evac,
                            rhs_wait=[("dv", att_dv)],
                            w2ap=bufA[:, 0:1024],
                            w2wait=[("pe", score1_pe[0])])
                flush_pending()
                x1_dv = cnt["dv"]

                def gen_ln(src, dst, store_rows=None, src_ready=None,
                           fwd_banks=(0, 1), back_banks=(2, 3), sq_banks=(4, 5),
                           post_back=None, back_dv_out=None, bufs=None):
                    if bufs is None:
                        bufs = x1tok2
                    x1f = x1tok_free_g
                    q_dv = {}
                    nrm_of = {}
                    nrm_q_of = {}
                    lastT_b = {}
                    x1n_mark = [0]

                    def fwd(c):
                        bi = c % len(bufs)
                        si = c % 2
                        xt = bufs[bi]
                        lastT = None
                        for kc in range(KC):
                            bnk = fwd_banks[(c * 4 + kc // 4) % 2]
                            w = [bank_free[bnk]] if kc % 4 == 0 else []
                            if kc == 0 and src_ready is not None:
                                w.append(src_ready)
                            lastT = emit("tensor", lambda e, kc=kc, c=c, bnk=bnk:
                                e.matmul(pb[bnk][0:128, (kc % 4) * 128:(kc % 4) * 128 + 128],
                                         src[:, kc * T + c * 128: kc * T + c * 128 + 128],
                                         i128[:], is_transpose=True,
                                         start=(kc % 4 == 0), stop=(kc % 4 == 3)),
                                waits=w, inc="pe")
                            if kc % 4 == 3:
                                ev = emit("vector", lambda e, kc=kc, bnk=bnk, xt=xt, bi=bi:
                                    e.tensor_scalar(xt[:, (kc // 4) * 512:(kc // 4) * 512 + 512],
                                                    pb[bnk][0:128, 0:512], 1.0, 0.0,
                                                    op0=ALU.mult, op1=ALU.add,
                                                    accum_out=s1q[si][:, kc // 4:kc // 4 + 1]),
                                    waits=[("pe", lastT)] + ([x1f[bi]] if kc == 3 else []),
                                    inc="dv")
                                bank_free[bnk] = ("dv", ev)
                                q_dv[kc // 4] = ev
                        tok_dv = cnt["dv"]
                        emit("vector", lambda e: e.drain())
                        s1_dv = emit("vector", lambda e, si=si:
                             e.reduce_sum(s1[si][:], s1q[si][:], axis=AX.X),
                             inc="dv")
                        sqb = sq_banks[c % 2]
                        sqm = None
                        for q4 in range(4):
                            w = [("dv", q_dv[q4])]
                            if q4 == 0:
                                w.append(bank_free[sqb])
                            else:
                                emit("scalar", lambda e: e.drain())
                            sqm = emit("scalar", lambda e, q4=q4, xt=xt, si=si, sqb=sqb:
                                       e.activation(out=pb[sqb][0:128, 0:512],
                                                    in_=xt[:, q4 * 512:(q4 + 1) * 512],
                                                    func=AF.Square,
                                                    accum_out=s2q[si][:, q4:q4 + 1]),
                                       waits=w, inc="se")
                        bank_free[sqb] = ("se", sqm)
                        emit("vector", lambda e: e.drain())
                        s2_dv = emit("vector", lambda e, si=si:
                             e.reduce_sum(s2[si][:], s2q[si][:], axis=AX.X),
                             waits=[("se", sqm)], inc="dv")
                        emit("scalar", lambda e: e.drain())
                        mu_se = emit("scalar", lambda e, si=si:
                                     e.activation(out=muv[si][:], in_=s1[si][:],
                                                  func=AF.Copy, scale=1.0 / D),
                                     waits=[("dv", s1_dv)], inc="se")
                        emit("scalar", lambda e: e.drain())
                        msq_se = emit("scalar", lambda e, si=si:
                                      e.activation(out=msq[si][:], in_=muv[si][:],
                                                   func=AF.Square),
                                      inc="se")
                        emit("vector", lambda e: e.drain())
                        ve = emit("vector", lambda e, si=si:
                                  e.scalar_tensor_tensor(varv[si][:], s2[si][:], 1.0 / D,
                                                         msq[si][:], op0=ALU.mult,
                                                         op1=ALU.subtract),
                                  waits=[("se", msq_se)], inc="dv")
                        emit("scalar", lambda e: e.drain())
                        sde = emit("scalar", lambda e, si=si:
                                   e.activation(out=sdv[si][:], in_=varv[si][:],
                                                func=AF.Sqrt, scale=1.0, bias=epsv[:]),
                                   waits=[("dv", ve), ("dv", eps_dv)], inc="se")
                        emit("vector", lambda e, si=si: e.reciprocal(rstd[si][:], sdv[si][:]),
                             waits=[("se", sde)])
                        emit("vector", lambda e: e.drain())
                        nrm = emit("vector", lambda e, xt=xt, si=si:
                                   e.tensor_scalar(xt, xt, muv[si][:], rstd[si][:],
                                                   op0=ALU.subtract, op1=ALU.mult),
                                   inc="dv")
                        nrm_of[c] = nrm
                        nrm_q_of[c] = [nrm, nrm, nrm, nrm]
                        if store_rows is not None:
                            ssem = "st" if bi % 2 == 0 else "su"
                            emit("gpsimd", lambda e, c=c, xt=xt:
                                 e.dma_start(out=d_out[store_rows + c * 128:
                                                       store_rows + c * 128 + 128, :],
                                             in_=xt),
                                 waits=[("dv", nrm_of[c])], inc=ssem, by=16)
                            x1f[bi] = (ssem, cnt[ssem])

                    def back(c):
                        bi = c % len(bufs)
                        xt = bufs[bi]
                        lastT = None
                        for kc in range(KC):
                            bnk = back_banks[(c * 4 + kc // 4) % 2]
                            w = [("dv", nrm_q_of[c][kc // 4])] if kc % 4 == 0 else []
                            if kc % 4 == 0:
                                w.append(bank_free[bnk])
                            lastT = emit("tensor", lambda e, kc=kc, bnk=bnk, xt=xt:
                                e.matmul(pb[bnk][0:128, (kc % 4) * 128:(kc % 4) * 128 + 128],
                                         xt[:, kc * 128:(kc + 1) * 128],
                                         i128[:], is_transpose=True,
                                         start=(kc % 4 == 0), stop=(kc % 4 == 3)),
                                waits=w, inc="pe")
                            if kc % 4 == 3:
                                ev = emit("vector", lambda e, kc=kc, c=c, bnk=bnk:
                                    e.tensor_copy(
                                        dst[:].rearrange("p (kk t) -> p kk t", kk=KC)
                                        [:, kc - 3:kc + 1, c * 128:(c + 1) * 128],
                                        pb[bnk][0:128, 0:512]
                                        .rearrange("p (kk t) -> p kk t", kk=4)),
                                    waits=[("pe", lastT)], inc="dv")
                                bank_free[bnk] = ("dv", ev)
                        x1f[bi] = ("pe", lastT)
                        if back_dv_out is not None:
                            back_dv_out[c] = cnt["dv"]
                        x1n_mark[0] = cnt["dv"]

                    if store_rows is not None:
                        for c in range(NCH):
                            fwd(c)
                        for c in range(NCH):
                            bi = c % len(bufs)
                            ssem = "st" if bi % 2 == 0 else "su"
                            x1f[bi] = (ssem, cnt[ssem])
                    else:
                        for s in range(NCH + 1):
                            if s < NCH:
                                fwd(s)
                            if s >= 1:
                                back(s - 1)
                                if post_back is not None:
                                    post_back(s - 1)
                    return x1n_mark[0]

                rbank = 6
                gbank = 7
                ge_m = {}; g2_m = {}; gxt_m = {}

                x1n_dv = gen_ln(xT, bufB, src_ready=("dv", x1_dv))
                x_free[0] = ("pe", cnt["pe"])

                emit("gpsimd", lambda e: e.dma_start(out=bb, in_=d_bb[:]),
                     waits=[("dv", x1n_dv), ("pe", cnt["pe"])], inc="lr", by=16)
                ffn_ld = cnt["lr"]
                emit("gpsimd", lambda e: e.dma_start(
                     out=laT.rearrange("p (kc r) -> p kc r", r=128),
                     in_=d_la[:].rearrange("(kc p) r -> p kc r", p=128)),
                     waits=[x1tok_free_g[1], ("dv", x1n_dv)],
                     inc="la", by=16)
                la_ld = cnt["la"]

                lastm = None
                rw_w = [("lg", rw_ld), ("dv", x1n_dv), bank_free[rbank]]
                for kc in range(KC):
                    lastm = emit("tensor", lambda e, kc=kc:
                        e.matmul(pb[rbank][0:8, 0:T],
                                 rwT[:, kc * 8:(kc + 1) * 8],
                                 bufB[:, kc * T:(kc + 1) * T],
                                 start=(kc == 0), stop=(kc == KC - 1)),
                        waits=(rw_w if kc == 0 else ()), inc="pe")
                lge = emit("vector", lambda e:
                           e.tensor_scalar(logT2[:, 0:T], pb[rbank][0:8, 0:T],
                                           rb2s, None, op0=ALU.add),
                           waits=[("pe", lastm), ("lg", rw_ld)], inc="dv")
                bank_free[rbank] = ("dv", lge)

                for c in range(NCH):
                    t = GT2[c]
                    lgt = emit("tensor", lambda e, c=c:
                        e.matmul(pb[gbank][0:128, 0:8],
                                 logT2[:, c * 128:(c + 1) * 128], i8[:],
                                 is_transpose=True, start=True, stop=True),
                        waits=[("dv", lge), bank_free[gbank]], inc="pe")
                    lev = emit("vector", lambda e, t=t:
                               e.tensor_copy(t["lgtok"], pb[gbank][0:128, 0:8]),
                               waits=[("pe", lgt)], inc="dv")
                    bank_free[gbank] = ("dv", lev)
                    emit("vector", lambda e: e.drain())
                    emit("vector", lambda e, t=t: e.reduce_max(t["m1"], t["lgtok"], axis=AX.X))
                    emit("vector", lambda e: e.drain())
                    emit("vector", lambda e, t=t: e.tensor_scalar(t["eq1"], t["lgtok"], t["m1"], None, op0=ALU.is_equal))
                    emit("vector", lambda e: e.drain())
                    emit("vector", lambda e, t=t: e.scalar_tensor_tensor(t["l2t"], t["eq1"], NEG_BIG, t["lgtok"], op0=ALU.mult, op1=ALU.add))
                    emit("vector", lambda e: e.drain())
                    emit("vector", lambda e, t=t: e.reduce_max(t["m2"], t["l2t"], axis=AX.X))
                    emit("vector", lambda e: e.drain())
                    g1 = emit("vector", lambda e, t=t: e.tensor_tensor(t["ddt"], t["m2"], t["m1"], op=ALU.subtract), inc="dv")
                    emit("vector", lambda e, t=t: e.tensor_scalar(t["eq2"], t["lgtok"], t["m2"], None, op0=ALU.is_equal))
                    ge_m[c] = emit("scalar", lambda e, t=t:
                                   e.activation(out=t["e2t"], in_=t["ddt"], func=AF.Exp),
                                   waits=[("dv", g1)], inc="se")

                for c in range(NCH):
                    t = GT2[c]
                    emit("vector", lambda e, t=t: e.tensor_scalar(t["dent"], t["e2t"], 1.0, None, op0=ALU.add),
                         waits=[("se", ge_m[c])])
                    emit("vector", lambda e: e.drain())
                    emit("vector", lambda e, t=t: e.reciprocal(t["rden"], t["dent"]))
                    emit("vector", lambda e: e.drain())
                    emit("vector", lambda e, t=t: e.tensor_scalar(t["w2s"], t["e2t"], t["rden"], 2.0, op0=ALU.mult, op1=ALU.mult))
                    emit("vector", lambda e, t=t: e.tensor_scalar(t["gatev"], t["eq1"], t["rden"], 2.0, op0=ALU.mult, op1=ALU.mult))
                    emit("vector", lambda e: e.drain())
                    emit("vector", lambda e, t=t: e.scalar_tensor_tensor(t["gatev"], t["eq2"], t["w2s"], t["gatev"], op0=ALU.mult, op1=ALU.add))
                    emit("vector", lambda e: e.drain())
                    g2_m[c] = emit("vector", lambda e, t=t, c=c:
                              e.tensor_copy(gexp2b[c].rearrange("p (e r) -> p e r", r=16),
                                            t["gatev"].unsqueeze(2).broadcast_to((128, 8, 16))),
                              inc="dv")

                labank = rbank
                lastm = None
                la_w = [("la", la_ld), ("dv", x1n_dv), bank_free[labank]]
                for kc in range(KC):
                    lastm = emit("tensor", lambda e, kc=kc:
                        e.matmul(pb[labank][0:128, 0:T],
                                 laT[:, kc * 128:(kc + 1) * 128],
                                 bufB[:, kc * T:(kc + 1) * T],
                                 start=(kc == 0), stop=(kc == KC - 1)),
                        waits=(la_w if kc == 0 else ()), inc="pe")
                la_pe = lastm

                for c in range(NCH):
                    w = [("dv", g2_m[c])]
                    if c == 0:
                        w.append(bank_free[gbank])
                    gxt_m[c] = emit("tensor", lambda e, c=c:
                        e.matmul(pb[gbank][0:128, c * 128:(c + 1) * 128],
                                 gexp2b[c], i128[:],
                                 is_transpose=True, start=(c == 0),
                                 stop=(c == NCH - 1)),
                        waits=w, inc="pe")

                bias_wait_l = [("lb", 0)]

                def base_mid():
                    emit("vector", lambda e:
                        e.tensor_copy(gateT2[:, 0:T], pb[gbank][0:128, 0:T]),
                        waits=[("pe", gxt_m[NCH - 1])], inc="dv")
                    gte_dv = cnt["dv"]
                    emit("vector", lambda e: e.drain())
                    gme = emit("vector", lambda e:
                        e.tensor_tensor(gmidT2[:, 0:T], pb[labank][0:128, 0:T],
                                        gateT2[:, 0:T], op=ALU.mult),
                        waits=[("pe", la_pe), ("dv", gte_dv)], inc="dv")
                    bank_free[labank] = ("dv", gme)
                    bank_free[gbank] = ("dv", gme)
                    bias_wait_l.append(("dv", gme))

                def x1nrhs(kc):
                    return bufB[:, kc * T:(kc + 1) * T]

                def lb_last(e, oc, ps_ap):
                    return e.matmul(ps_ap, lbF[:, oc * 128:(oc + 1) * 128],
                                    gmidT2[:, 0:T], start=False, stop=True)

                def base_evac(e, oc, ps):
                    return e.scalar_tensor_tensor(bufC[:, oc * T:(oc + 1) * T],
                                                  ps, bb[:, oc:oc + 1],
                                                  bufB[:, oc * T:(oc + 1) * T],
                                                  op0=ALU.add, op1=ALU.add)

                emit("gpsimd", lambda e: e.dma_start(out=lbF, in_=d_lb[:]),
                     waits=[("pe", cnt["pe"])], inc="lb", by=16)
                lb_ld = cnt["lb"]
                bias_wait_l[0] = ("lb", lb_ld)
                base_evac.guard = [("lr", ffn_ld)]
                stream_gemm(w_slice(d_bw), x1nrhs, base_evac,
                            rhs_wait=[("dv", x1n_dv)], bias_last=lb_last,
                            bias_wait=bias_wait_l, mid_cb=base_mid,
                            w2ap=bufA[:, 0:1024])
                flush_pending()
                x2_dv = cnt["dv"]
                bufB_guard = [("se", cnt["se"]), ("dv", cnt["dv"])]

                ln2_bufs = [x1tok2[0], x1tok2[1]]
                if NCH > 2:
                    ln2_bufs += [avT[:, 0:2048], avT[:, 2048:4096]]
                gen_ln(bufC, None, store_rows=t0, src_ready=("dv", x2_dv),
                       fwd_banks=(4, 5), sq_banks=(6, 7), bufs=ln2_bufs)
                bufC_guard = [("pe", cnt["pe"]), ("se", cnt["se"])]

            emit("gpsimd", lambda e: e.nop(),
                 waits=[("st", cnt["st"]), ("su", cnt["su"])])

        prev_av = [None, None]

        def run_engine(eng_name, E):
            prev_av[0] = None
            prev_av[1] = None
            program(eng_name, E)

        @block.sync
        def _(sync):
            run_engine("sync", sync)

        @block.tensor
        def _(tensor):
            run_engine("tensor", tensor)

        @block.vector
        def _(vector):
            run_engine("vector", vector)

        @block.scalar
        def _(scalar):
            run_engine("scalar", scalar)

        @block.gpsimd
        def _(gpsimd):
            run_engine("gpsimd", gpsimd)

    return nc


def _prep_inputs(L0_attn_x, L0_attn_Wq, L0_attn_Wk, L0_attn_Wv, L0_attn_Wo,
                 L0_ffn_base_w, L0_ffn_base_b, L0_ffn_router_w, L0_ffn_router_b,
                 L0_ffn_lora_A, L0_ffn_lora_B, n_cores, shard_t):
    x2 = np.ascontiguousarray(np.asarray(L0_attn_x, dtype=np.float32)[0])
    xT = np.ascontiguousarray(x2.T)
    base = {
        "wqT": np.ascontiguousarray(np.asarray(L0_attn_Wq, np.float32).T),
        "wkT": np.ascontiguousarray(np.asarray(L0_attn_Wk, np.float32).T),
        "wvT": np.ascontiguousarray(np.asarray(L0_attn_Wv, np.float32).T),
        "woT": np.ascontiguousarray(np.asarray(L0_attn_Wo, np.float32).T),
        "bwT": np.ascontiguousarray(np.asarray(L0_ffn_base_w, np.float32).T),
        "laT": np.ascontiguousarray(
            np.asarray(L0_ffn_lora_A, np.float32).transpose(1, 0, 2).reshape(D, 128)),
        "lbF": np.ascontiguousarray(np.asarray(L0_ffn_lora_B, np.float32).reshape(128, D)),
        "rwT": np.ascontiguousarray(np.asarray(L0_ffn_router_w, np.float32).T),
        "bb2": np.ascontiguousarray(np.asarray(L0_ffn_base_b, np.float32).reshape(KC, 128).T),
        "rb2": np.ascontiguousarray(np.asarray(L0_ffn_router_b, np.float32).reshape(8, 1)),
        "id128": np.eye(128, dtype=np.float32),
        "id16": np.eye(16, dtype=np.float32),
        "id8": np.eye(8, dtype=np.float32),
        "maskbd": np.kron(np.eye(8, dtype=np.float32),
                          np.ones((16, 16), dtype=np.float32)),
    }
    in_maps = []
    for c in range(n_cores):
        m = dict(base)
        m["xT"] = np.ascontiguousarray(xT[:, c * shard_t:(c + 1) * shard_t])
        in_maps.append(m)
    return in_maps


def kernel(**inputs):
    n_cores = 8
    S = inputs["L0_attn_x"].shape[1]
    shard_t = S // n_cores
    nc = build_nc(shard_t=shard_t, tile_t=512)
    in_maps = _prep_inputs(n_cores=n_cores, shard_t=shard_t, **inputs)
    res = run_bass_kernel_spmd(nc, in_maps, core_ids=list(range(n_cores)))
    out = np.concatenate([r["out"] for r in res.results], axis=0)
    return out.reshape(1, S, D)
